# version 02
# baseline (speedup 1.0000x reference)
import numpy as np
import ml_dtypes
import concourse.bass as bass
import concourse.mybir as mybir
from concourse.bass_utils import run_bass_kernel_spmd

F32 = mybir.dt.float32
BF16 = mybir.dt.bfloat16
I32 = mybir.dt.int32
AF = mybir.ActivationFunctionType
ALU = mybir.AluOpType
AX = mybir.AxisListType

NCORES = 8
BPC = 4
T = 2048
E = 1024
HID = 3072
NHEAD = 16
D = 64
NEG = -1.0e9
NTILE = T // 128
FTILE = 256
NFT = (BPC * T) // FTILE

ENGS = ("sync", "gpsimd", "scalar", "vector", "tensor")


class Em:

    def __init__(self):
        self.ops = {e: [] for e in ENGS}
        self.cnt = {}

    def sem(self, *names):
        for n in names:
            self.cnt[n] = 0

    def op(self, eng, fn, wait=(), inc=None, inc_amt=1):
        w = tuple((s, v) for (s, v) in wait if v > 0)
        if inc is not None:
            self.cnt[inc] += inc_amt
        self.ops[eng].append((w, fn, inc, inc_amt))
        return self.cnt[inc] if inc is not None else None

    def now(self, sem):
        return self.cnt[sem]

    def replay(self, eng_name, engobj, semh):
        last = {}
        for (waits, fn, inc, amt) in self.ops[eng_name]:
            for (s, v) in waits:
                if last.get(s, 0) < v:
                    engobj.wait_ge(semh[s], v)
                    last[s] = v
            inst = fn(engobj) if fn is not None else None
            if inc is not None:
                assert inst is not None
                inst.then_inc(semh[inc], amt)


def build_program():
    nc = bass.Bass("TRN2", target_bir_lowering=False)

    x_d = nc.dram_tensor("x", [BPC, T, E], F32, kind="ExternalInput").ap()
    gum_d = nc.dram_tensor("gum", [BPC, T, T], F32, kind="ExternalInput").ap()
    wqkT_d = nc.dram_tensor("wqkT", [E, 2 * E], F32, kind="ExternalInput").ap()
    bias_d = nc.dram_tensor("biasqk", [2 * E], F32, kind="ExternalInput").ap()
    wgT_d = nc.dram_tensor("wgT", [E, HID], I32, kind="ExternalInput").ap()
    wuT_d = nc.dram_tensor("wuT", [E, HID], I32, kind="ExternalInput").ap()
    wdT_d = nc.dram_tensor("wdT", [HID, E], I32, kind="ExternalInput").ap()
    ident_d = nc.dram_tensor("ident", [128, 128], BF16, kind="ExternalInput").ap()
    mask_d = nc.dram_tensor("maskneg", [128, 128], BF16, kind="ExternalInput").ap()
    ones_d = nc.dram_tensor("ones1", [1, 128], F32, kind="ExternalInput").ap()
    out_d = nc.dram_tensor("out", [BPC, T, E], F32, kind="ExternalOutput").ap()
    xn_scr = nc.dram_tensor("xn_scr", [BPC * T, E], BF16, kind="Internal").ap()
    g_scr = nc.dram_tensor("g_scr", [BPC * T, E], BF16, kind="Internal").ap()

    ARENA_BYTES = 209088
    arena_h = nc.alloc_sbuf_tensor("arena", [128, ARENA_BYTES], mybir.dt.uint8)
    ARBASE = nc.lookup_mls(arena_h).memorylocations[0].addr
    alloc = {}

    def sb(name, shape, dtype, off):
        h = nc.alloc_sbuf_tensor_at(name, shape, dtype, offset=ARBASE + off)
        alloc[name] = h
        return h.ap()

    GLOB0 = 0
    ident = sb("identsb", [128, 128], BF16, GLOB0 + 0)
    maskneg = sb("masksb", [128, 128], BF16, GLOB0 + 256)
    ones1 = sb("onessb", [1, 128], F32, GLOB0 + 512)
    biasS = sb("biasS", [128, 16], F32, GLOB0 + 1024)
    biasRaw = sb("biasRaw", [128, 16], F32, GLOB0 + 1088)
    bias1 = sb("bias1", [128, 1], F32, GLOB0 + 1152)
    smallA = sb("smallA", [128, 48], F32, GLOB0 + 1184)
    sc1 = sb("sc1", [1, 16], F32, GLOB0 + 1376)
    AR = 2048

    wg_bf = sb("wg_bf", [128, 8, HID], BF16, AR + 0)
    wu_bf = sb("wu_bf", [128, 8, HID], BF16, AR + 49152)
    wd_bf = sb("wd_bf", [128, 24, E], BF16, AR + 98304)
    x_f = [sb(f"x_f{j}", [128, 2, E], F32, AR + 147456 + j * 8192) for j in range(2)]
    xn_f = [sb(f"xn_f{j}", [128, 2, E], BF16, AR + 163840 + j * 4096) for j in range(2)]
    xnT_f = sb("xnT_f", [128, 8, FTILE], BF16, AR + 172032)
    a_strip = sb("a_strip", [128, FTILE], BF16, AR + 176128)
    u_strip = sb("u_strip", [128, FTILE], BF16, AR + 176640)
    h_bf = sb("h_bf", [128, 24, FTILE], BF16, AR + 177152)
    g_f = [sb(f"g_f{j}", [128, 2, E], BF16, AR + 189440 + j * 4096) for j in range(2)]
    junk = sb("junk", [128, E], BF16, AR + 197632)
    stg = [sb(f"stg{j}", [128, HID], I32, AR + 147456 + j * 12288) for j in range(2)]

    wqk_bf = sb("wqk_bf", [128, 8, 2 * E], BF16, AR + 0)
    QT = sb("QT", [128, 8, T], BF16, AR + 32768)
    KT = sb("KT", [128, 8, T], BF16, AR + 65536)
    xn_a = sb("xn_a", [128, NTILE, E], BF16, AR + 98304)
    xnT_tb = sb("xnT_tb", [128, 8, 512], BF16, AR + 131072)
    avg = sb("avg", [128, T], F32, AR + 139264)
    E_buf = [sb(f"E_buf{j}", [128, T], BF16, AR + 147456 + j * 4096) for j in range(2)]
    l2_buf = [sb(f"l2_buf{j}", [128, T], BF16, AR + 155648 + j * 4096) for j in range(2)]
    w_buf = [sb(f"w_buf{j}", [128, T], BF16, AR + 163840 + j * 4096) for j in range(2)]
    wT_sb = sb("wT_sb", [128, NTILE, 128], BF16, AR + 172032)
    gum_b = [sb(f"gum_b{j}", [128, T], F32, AR + 176128 + j * 8192) for j in range(2)]
    xres = [sb(f"xres{j}", [128, E], F32, AR + 192512 + j * 4096) for j in range(2)]
    gt_b = [sb(f"gt_b{j}", [128, E], BF16, AR + 200704 + j * 2048) for j in range(2)]
    sg = sb("sg", [128, E], BF16, AR + 204800)
    Zp = sb("Zp", [128, 4], F32, AR + 206848)
    cvec = sb("cvec", [128, 1], F32, AR + 206880)
    z01 = sb("z01", [128, 1], F32, AR + 206912)
    Z2 = sb("Z2", [128, 2], F32, AR + 206944)
    rec2 = sb("rec2", [128, 1], F32, AR + 206976)
    wstg = [sb(f"wstg{j}", [128, 2 * E], F32, AR + 32768 + j * 8192) for j in range(2)]

    ps_guards = []

    def ps(name, shape):
        g = nc.psum_tensor(name, shape, F32)
        h = g.__enter__()
        ps_guards.append(g)
        return h.ap()

    big0 = ps("big0", [128, 1024])
    big1 = ps("big1", [128, 1024])
    big2 = ps("big2", [128, 1024])
    small0 = ps("small0", [128, 512])
    small1 = ps("small1", [128, 512])
    psA = small0[:, 0:FTILE]
    psB = small1[:, 0:FTILE]
    psF = [big0, big1]
    ptF = [big2[:, 0:128], big2[:, 512:640]]
    region = [big0, big1]
    sOut = big2
    ptA = [small0[:, 0:128], small1[:, 0:128]]

    em = Em()
    SEMS = ("Lc", "Lx0", "Lx1", "Ls0", "Ls1", "Lxn", "Lg0", "Lg1", "Lr0", "Lr1",
            "Lt0", "Lt1", "Lw0", "Lw1", "Lb", "Sxn0", "Sxn1", "Sg0", "Sg1",
            "So0", "So1", "P", "A", "V")
    em.sem(*SEMS)

    em.op("sync", lambda e: e.dma_start(out=ident[:], in_=ident_d[:]), inc="Lc", inc_amt=16)
    em.op("sync", lambda e: e.dma_start(out=maskneg[:], in_=mask_d[:]), inc="Lc", inc_amt=16)
    em.op("sync", lambda e: e.dma_start(out=ones1[:], in_=ones_d[:]), inc="Lc", inc_amt=16)
    L_consts = em.now("Lc")

    cast_jobs = []
    for c in range(8):
        cast_jobs.append((wgT_d[c * 128:(c + 1) * 128, :], wg_bf[:, c, :], HID))
    for c in range(8):
        cast_jobs.append((wuT_d[c * 128:(c + 1) * 128, :], wu_bf[:, c, :], HID))
    for c in range(24):
        cast_jobs.append((wdT_d[c * 128:(c + 1) * 128, :], wd_bf[:, c, :], E))
    stg_free = [("V", 0), ("V", 0)]
    for j, (src, dst, ncols) in enumerate(cast_jobs):
        sl = j % 2
        em.op("sync", (lambda e, s=src, t=stg[sl][:, 0:ncols]: e.dma_start(out=t, in_=s)),
              wait=[stg_free[sl]], inc=f"Ls{sl}", inc_amt=16)
        lv = em.now(f"Ls{sl}")
        em.op("vector", (lambda e, s=stg[sl][:, 0:ncols], t=dst: e.tensor_copy(out=t, in_=s)),
              wait=[(f"Ls{sl}", lv)], inc="V")
        stg_free[sl] = ("V", em.now("V"))
    V_wcast = em.now("V")

    xf_free = [("V", 0), ("V", 0)]
    gf_free = [("Sg0", 0), ("Sg1", 0)]
    xn_free = [("Sxn0", 0), ("Sxn1", 0)]
    psF_free = ("A", 0)
    psAB_free = [("A", 0), ("A", 0)]
    h_free = ("P", 0)
    au_free = ("V", 0)
    ptF_free = [("V", 0), ("V", 0)]
    NSUB = FTILE // 128

    for ft in range(NFT):
        b = ft // (T // FTILE)
        t0 = (ft % (T // FTILE)) * FTILE
        xb = ft % 2
        w_extra = [("V", V_wcast)] if ft < 2 else []
        em.op("sync", (lambda e, b=b, t0=t0, xb=xb:
                       e.dma_start(out=x_f[xb][:],
                                   in_=x_d[b, t0:t0 + FTILE, :].rearrange("(s p) e -> p s e", p=128))),
              wait=[xf_free[xb]] + w_extra, inc=f"Lx{xb}", inc_amt=16)
        L_x = em.now(f"Lx{xb}")

        def vop(fn, wait=()):
            nonlocal vprev
            em.op("vector", fn, wait=list(wait) + [("V", vprev)], inc="V")
            vprev = em.now("V")
        vprev = em.now("V")
        wv = [(f"Lx{xb}", L_x)] + ([("V", V_wcast)] if ft < 2 else []) + [xn_free[xb]]
        m_v = smallA[:, 0:2]
        q2_v = smallA[:, 2:4]
        var_v = smallA[:, 4:6]
        r_v = smallA[:, 6:8]
        tmp_v = smallA[:, 8:10]
        tmp2_v = smallA[:, 10:12]
        for s in range(NSUB):
            vop((lambda e, s=s, xb=xb: e.reduce_sum(
                out=m_v[:, s:s + 1], in_=x_f[xb][:, s, :], axis=AX.X)), wait=wv)
            wv = []
            vop((lambda e, s=s, xb=xb: e.scalar_tensor_tensor(
                out=junk[:], in0=x_f[xb][:, s, :], scalar=1.0, in1=x_f[xb][:, s, :],
                op0=ALU.mult, op1=ALU.mult, accum_out=q2_v[:, s:s + 1])))
        vop(lambda e: e.tensor_scalar(
            out=m_v[:], in0=m_v[:], scalar1=1.0 / E, scalar2=None, op0=ALU.mult))
        vop(lambda e: e.tensor_tensor(
            out=tmp_v[:], in0=m_v[:], in1=m_v[:], op=ALU.mult))
        vop(lambda e: e.tensor_scalar(
            out=var_v[:], in0=q2_v[:], scalar1=1.0 / E, scalar2=None, op0=ALU.mult))
        vop(lambda e: e.tensor_tensor(
            out=var_v[:], in0=var_v[:], in1=tmp_v[:], op=ALU.subtract))
        vop(lambda e: e.tensor_scalar(
            out=var_v[:], in0=var_v[:], scalar1=1e-5, scalar2=None, op0=ALU.add))
        vop(lambda e: e.memset(r_v[:], 1.0))
        for _ in range(5):
            vop(lambda e: e.tensor_tensor(
                out=tmp_v[:], in0=var_v[:], in1=r_v[:], op=ALU.mult))
            vop(lambda e: e.tensor_tensor(
                out=tmp2_v[:], in0=tmp_v[:], in1=r_v[:], op=ALU.mult))
            vop(lambda e: e.tensor_scalar(
                out=tmp2_v[:], in0=tmp2_v[:], scalar1=-0.5, scalar2=1.5,
                op0=ALU.mult, op1=ALU.add))
            vop(lambda e: e.tensor_tensor(
                out=r_v[:], in0=r_v[:], in1=tmp2_v[:], op=ALU.mult))
        for s in range(NSUB):
            vop((lambda e, s=s, xb=xb: e.tensor_scalar(
                out=xn_f[xb][:, s, :], in0=x_f[xb][:, s, :],
                scalar1=m_v[:, s:s + 1], scalar2=r_v[:, s:s + 1],
                op0=ALU.subtract, op1=ALU.mult)))
        V_xn = em.now("V")
        xf_free[xb] = ("V", V_xn)

        em.op("gpsimd", (lambda e, b=b, t0=t0, xb=xb: e.dma_start(
            out=xn_scr[b * T + t0:b * T + t0 + FTILE, :].rearrange("(s p) e -> p s e", p=128),
            in_=xn_f[xb][:])), wait=[("V", V_xn)], inc=f"Sxn{xb}", inc_amt=16)
        xn_free[xb] = (f"Sxn{xb}", em.now(f"Sxn{xb}"))

        for s in range(NSUB):
            for c in range(8):
                k = s * 8 + c
                sl = k % 2
                em.op("tensor", (lambda e, s=s, c=c, sl=sl, xb=xb: e.matmul(
                    ptF[sl][:], xn_f[xb][:, s, c * 128:(c + 1) * 128], ident[:],
                    start=True, stop=True)),
                    wait=[("V", V_xn), ptF_free[sl], ("Lc", 48)], inc="P")
                pv = em.now("P")
                em.op("vector", (lambda e, s=s, c=c, sl=sl: e.tensor_copy(
                    out=xnT_f[:, c, s * 128:(s + 1) * 128], in_=ptF[sl][:])),
                    wait=[("P", pv)], inc="V")
                ptF_free[sl] = ("V", em.now("V"))
        V_xnT = em.now("V")

        for m in range(24):
            for ec in range(8):
                em.op("tensor", (lambda e, m=m, ec=ec: e.matmul(
                    psA[:], wg_bf[:, ec, m * 128:(m + 1) * 128], xnT_f[:, ec, :],
                    start=(ec == 0), stop=(ec == 7))),
                    wait=([("V", V_xnT), psAB_free[0]] if ec == 0 else []),
                    inc="P" if ec == 7 else None)
            P_gate = em.now("P")
            for ec in range(8):
                em.op("tensor", (lambda e, m=m, ec=ec: e.matmul(
                    psB[:], wu_bf[:, ec, m * 128:(m + 1) * 128], xnT_f[:, ec, :],
                    start=(ec == 0), stop=(ec == 7))),
                    wait=([psAB_free[1]] if ec == 0 else []),
                    inc="P" if ec == 7 else None)
            P_up = em.now("P")
            em.op("scalar", (lambda e: e.activation(
                out=a_strip[:], in_=psA[:], func=AF.Silu, scale=1.0 / 127.0)),
                wait=[("P", P_gate), au_free], inc="A")
            psAB_free[0] = ("A", em.now("A"))
            em.op("scalar", (lambda e: e.activation(
                out=u_strip[:], in_=psB[:], func=AF.Copy)),
                wait=[("P", P_up)], inc="A")
            psAB_free[1] = ("A", em.now("A"))
            A_u = em.now("A")
            em.op("vector", (lambda e, m=m: e.tensor_tensor(
                out=h_bf[:, m, :], in0=a_strip[:], in1=u_strip[:], op=ALU.mult)),
                wait=[("A", A_u), (h_free if m == 0 else ("P", 0))], inc="V")
            au_free = ("V", em.now("V"))
        V_h = em.now("V")

        for m in range(24):
            vm = V_h - 23 + m
            for s2 in range(NSUB):
                for n2 in range(2):
                    em.op("tensor", (lambda e, m=m, s2=s2, n2=n2: e.matmul(
                        psF[s2][:, n2 * 512:(n2 + 1) * 512],
                        h_bf[:, m, s2 * 128:(s2 + 1) * 128],
                        wd_bf[:, m, n2 * 512:(n2 + 1) * 512],
                        start=(m == 0), stop=(m == 23))),
                        wait=([("V", vm)] + ([psF_free] if m == 0 and s2 == 0 and n2 == 0 else [])),
                        inc="P" if (m == 23 and s2 == NSUB - 1 and n2 == 1) else None)
        P_down = em.now("P")
        h_free = ("P", P_down)

        gb = ft % 2
        for s2 in range(NSUB):
            em.op("scalar", (lambda e, s2=s2, gb=gb: e.activation(
                out=g_f[gb][:, s2, :], in_=psF[s2][:], func=AF.Silu,
                scale=1.0 / (127.0 * 127.0))),
                wait=[("P", P_down)] + ([gf_free[gb]] if s2 == 0 else []),
                inc="A" if s2 == NSUB - 1 else None)
        A_g = em.now("A")
        psF_free = ("A", A_g)
        em.op("gpsimd", (lambda e, b=b, t0=t0, gb=gb: e.dma_start(
            out=g_scr[b * T + t0:b * T + t0 + FTILE, :].rearrange("(s p) e -> p s e", p=128),
            in_=g_f[gb][:])), wait=[("A", A_g)], inc=f"Sg{gb}", inc_amt=16)
        gf_free[gb] = (f"Sg{gb}", em.now(f"Sg{gb}"))

    Fbar = [("P", em.now("P")), ("A", em.now("A")), ("V", em.now("V")),
            ("Sxn0", em.now("Sxn0")), ("Sxn1", em.now("Sxn1")),
            ("Sg0", em.now("Sg0")), ("Sg1", em.now("Sg1"))]

    wstg_free = [("V", 0), ("V", 0)]
    for c in range(8):
        sl = c % 2
        em.op("sync", (lambda e, c=c, sl=sl: e.dma_start(
            out=wstg[sl][:], in_=wqkT_d[c * 128:(c + 1) * 128, :])),
            wait=(Fbar if c < 2 else [wstg_free[sl]]), inc=f"Lw{sl}", inc_amt=16)
        lv = em.now(f"Lw{sl}")
        em.op("vector", (lambda e, c=c, sl=sl: e.tensor_copy(
            out=wqk_bf[:, c, :], in_=wstg[sl][:])),
            wait=[(f"Lw{sl}", lv)] + (Fbar if c < 2 else []), inc="V")
        wstg_free[sl] = ("V", em.now("V"))
    for c in range(16):
        em.op("sync", (lambda e, c=c: e.dma_start(
            out=biasRaw[:, c:c + 1],
            in_=bias_d[c * 128:(c + 1) * 128].rearrange("(p o) -> p o", o=1))),
            inc="Lb", inc_amt=16)
    L_bias = em.now("Lb")
    em.op("vector", lambda e: e.tensor_scalar(
        out=biasS[:, 0:8], in0=biasRaw[:, 0:8], scalar1=0.125, scalar2=None, op0=ALU.mult),
        wait=[("Lb", L_bias)])
    em.op("vector", lambda e: e.tensor_copy(out=biasS[:, 8:16], in_=biasRaw[:, 8:16]),
          inc="V")
    V_bias = em.now("V")
    V_wqk = em.now("V")

    xn_a_lastuse = ("P", 0)
    sOut_slot_free = [("V", 0), ("V", 0)]
    pt_free = [("V", 0), ("V", 0)]
    regions_free = [("A", 0), ("A", 0)]
    E_free = [("V", 0), ("V", 0)]
    l2_free = [("A", 0), ("A", 0)]
    w_free = [("P", 0), ("P", 0)]
    wT_free = ("P", 0)
    gum_free = [("V", 0), ("V", 0)]
    xres_free = [("S", 0), ("S", 0)]
    gt_free = [("V", 0), ("V", 0)]
    avg_free = ("V", 0)
    last_stt = 0
    last_comb = 0

    for b in range(BPC):
        em.op("sync", (lambda e, b=b: e.dma_start(
            out=xn_a[:],
            in_=xn_scr[b * T:(b + 1) * T, :].rearrange("(tt p) e -> p tt e", p=128))),
            wait=[xn_a_lastuse] + (Fbar if b == 0 else []), inc="Lxn", inc_amt=16)
        L_xn = em.now("Lxn")

        for tb in range(4):
            for k in range(32):
                tt = tb * 4 + k // 8
                ec = k % 8
                sl = k % 2
                em.op("tensor", (lambda e, tt=tt, ec=ec, sl=sl: e.matmul(
                    ptA[sl][:], xn_a[:, tt, ec * 128:(ec + 1) * 128], ident[:],
                    start=True, stop=True)),
                    wait=[("Lxn", L_xn), pt_free[sl], ("V", V_wqk)],
                    inc="P")
                pv = em.now("P")
                em.op("vector", (lambda e, tt=tt, ec=ec, sl=sl: e.tensor_copy(
                    out=xnT_tb[:, ec, (tt % 4) * 128:(tt % 4 + 1) * 128], in_=ptA[sl][:])),
                    wait=[("P", pv)], inc="V")
                pt_free[sl] = ("V", em.now("V"))
            V_tb = em.now("V")
            for co in range(16):
                sl = co % 2
                for ec in range(8):
                    em.op("tensor", (lambda e, co=co, ec=ec, sl=sl: e.matmul(
                        sOut[:, sl * 512:(sl + 1) * 512],
                        wqk_bf[:, ec, co * 128:(co + 1) * 128],
                        xnT_tb[:, ec, :],
                        start=(ec == 0), stop=(ec == 7))),
                        wait=([("V", V_tb), sOut_slot_free[sl]] if ec == 0 else []),
                        inc="P" if ec == 7 else None)
                P_co = em.now("P")
                dstQK = QT if co < 8 else KT
                cc = co % 8
                scale = 0.125 if co < 8 else 1.0
                em.op("scalar", (lambda e, co=co, sl=sl, dstQK=dstQK, cc=cc, tb=tb, scale=scale:
                                 e.activation(
                    out=dstQK[:, cc, tb * 512:(tb + 1) * 512],
                    in_=sOut[:, sl * 512:(sl + 1) * 512], func=AF.Identity,
                    bias=biasS[:, co:co + 1], scale=scale)),
                    wait=[("P", P_co), ("V", V_bias)], inc="A")
                sOut_slot_free[sl] = ("A", em.now("A"))
        A_proj = em.now("A")

        qm = smallA[:, 16:17]
        km = smallA[:, 17:18]
        qm_bf = smallA[:, 18:19].bitcast(BF16)[:, 0:1]
        km_bf = smallA[:, 20:21].bitcast(BF16)[:, 0:1]
        def nvop(fn, wait=()):
            nonlocal nbprev
            em.op("vector", fn, wait=list(wait) + [("V", nbprev)], inc="V")
            nbprev = em.now("V")
        nbprev = em.now("V")
        nvop(lambda e: e.reduce_max(
            out=qm[:], in_=QT.rearrange("p c t -> p (c t)"), axis=AX.X,
            apply_absolute_value=True), wait=[("A", A_proj)])
        nvop(lambda e: e.reduce_max(
            out=km[:], in_=KT.rearrange("p c t -> p (c t)"), axis=AX.X,
            apply_absolute_value=True))
        nvop(lambda e: e.tensor_copy(out=qm_bf[:], in_=qm[:]))
        nvop(lambda e: e.tensor_copy(out=km_bf[:], in_=km[:]))
        V_qk = em.now("V")
        em.op("tensor", lambda e: e.matmul(ptA[0][0:1, :], qm_bf[:], ident[:],
                                           start=True, stop=True),
              wait=[("V", V_qk), pt_free[0]], inc="P")
        P_q = em.now("P")
        em.op("tensor", lambda e: e.matmul(ptA[1][0:1, :], km_bf[:], ident[:],
                                           start=True, stop=True),
              wait=[pt_free[1]], inc="P")
        P_k = em.now("P")
        nvop(lambda e: e.reduce_max(out=sc1[:, 0:1], in_=ptA[0][0:1, :], axis=AX.X),
             wait=[("P", P_q)])
        nvop(lambda e: e.reduce_max(out=sc1[:, 1:2], in_=ptA[1][0:1, :], axis=AX.X),
             wait=[("P", P_k)])
        nvop(lambda e: e.tensor_tensor(
            out=sc1[:, 2:3], in0=sc1[:, 0:1], in1=sc1[:, 1:2], op=ALU.mult))
        nvop(lambda e: e.tensor_scalar(
            out=sc1[:, 2:3], in0=sc1[:, 2:3], scalar1=64.0, scalar2=80.0,
            op0=ALU.mult, op1=ALU.min))
        nvop(lambda e: e.tensor_scalar(
            out=sc1[:, 2:3], in0=sc1[:, 2:3], scalar1=-1.0, scalar2=None, op0=ALU.mult))
        V_nb = em.now("V")
        pt_free[0] = ("V", V_nb)
        pt_free[1] = ("V", V_nb)
        em.op("tensor", lambda e: e.matmul(ptA[0][:, 0:1], ones1[:], sc1[:, 2:3],
                                           start=True, stop=True),
              wait=[("V", V_nb)], inc="P")
        P_b = em.now("P")
        em.op("vector", lambda e: e.tensor_copy(out=bias1[:], in_=ptA[0][:, 0:1]),
              wait=[("P", P_b)], inc="V")
        pt_free[0] = ("V", em.now("V"))
        V_bias1 = em.now("V")

        for i in range(NTILE):
            Lk = 128 * (i + 1)
            ib = i % 2
            em.op("sync", (lambda e, b=b, i=i, Lk=Lk, ib=ib: e.dma_start(
                out=gum_b[ib][:, 0:Lk], in_=gum_d[b, i * 128:(i + 1) * 128, 0:Lk])),
                wait=[gum_free[ib]], inc=f"Lg{ib}", inc_amt=16)
            L_gum = em.now(f"Lg{ib}")
            em.op("sync", (lambda e, b=b, i=i, ib=ib: e.dma_start(
                out=xres[ib][:], in_=x_d[b, i * 128:(i + 1) * 128, :])),
                wait=[xres_free[ib]], inc=f"Lr{ib}", inc_amt=16)
            L_xres = em.now(f"Lr{ib}")
            em.op("sync", (lambda e, b=b, i=i, ib=ib: e.dma_start(
                out=gt_b[ib][:], in_=g_scr[b * T + i * 128:b * T + (i + 1) * 128, :])),
                wait=[gt_free[ib]], inc=f"Lt{ib}", inc_amt=16)
            L_gt = em.now(f"Lt{ib}")

            n_d = (Lk - 128) // 512
            d_off = (Lk - 128) - 512 * n_d
            nchunks = (Lk + 511) // 512
            for h in range(NHEAD):
                hp = h % 2
                R = region[hp]
                qb = 64 * (h % 2)
                qc = h // 2
                parts = [(0, min(Lk, 1024))] + ([(1024, Lk)] if Lk > 1024 else [])
                for (pi, (c0, c1)) in enumerate(parts):
                    nlo = c0 // 512
                    nhi = (c1 + 511) // 512
                    for n in range(nlo, nhi):
                        k0 = 512 * n
                        w_n = min(512, Lk - k0)
                        rs = k0 - c0
                        is_diag = (n == n_d)
                        last = (n == nhi - 1)
                        wv = []
                        if n == nlo:
                            wv.append(regions_free[hp])
                            if h == 0 and pi == 0:
                                wv.append(("A", A_proj))
                        em.op("tensor", (lambda e, qb=qb, qc=qc, i=i, k0=k0, w_n=w_n, rs=rs,
                                         R=R, is_diag=is_diag: e.matmul(
                            R[:, rs:rs + w_n],
                            QT[qb:qb + 64, qc, i * 128:(i + 1) * 128],
                            KT[qb:qb + 64, qc, k0:k0 + w_n],
                            start=True, stop=not is_diag)),
                            wait=wv,
                            inc="P" if (last and not is_diag) else None)
                        if is_diag:
                            em.op("tensor", (lambda e, R=R, rs=rs, w_n=w_n: e.matmul(
                                R[:, rs + w_n - 128:rs + w_n],
                                ident[:], maskneg[:],
                                start=False, stop=True)),
                                inc="P" if last else None)
                    P_sc = em.now("P")
                    zc = hp * 2 + pi
                    em.op("scalar", (lambda e, R=R, c0=c0, c1=c1, hp=hp, zc=zc: e.activation(
                        out=E_buf[hp][:, c0:c1], in_=R[:, 0:c1 - c0], func=AF.Exp,
                        bias=bias1[:], scale=1.0,
                        accum_out=Zp[:, zc:zc + 1])),
                        wait=[("P", P_sc), ("V", V_bias1)] + ([E_free[hp]] if pi == 0 else []),
                        inc="A")
                    regions_free[hp] = ("A", em.now("A"))
                A_exp = em.now("A")
                if len(parts) == 2:
                    em.op("vector", (lambda e, hp=hp: e.tensor_tensor(
                        out=z01[:], in0=Zp[:, 2 * hp:2 * hp + 1],
                        in1=Zp[:, 2 * hp + 1:2 * hp + 2], op=ALU.add)),
                        wait=[("A", A_exp), ("V", last_stt)], inc="V")
                else:
                    em.op("vector", (lambda e, hp=hp: e.tensor_copy(
                        out=z01[:], in_=Zp[:, 2 * hp:2 * hp + 1])),
                        wait=[("A", A_exp), ("V", last_stt)], inc="V")
                em.op("vector", lambda e: e.tensor_scalar(
                    out=z01[:], in0=z01[:], scalar1=1e-37, scalar2=None, op0=ALU.max),
                    wait=[("V", em.now("V"))], inc="V")
                em.op("vector", lambda e: e.reciprocal(out=cvec[:], in_=z01[:]),
                      wait=[("V", em.now("V"))], inc="V")
                V_c = em.now("V")
                if h == 0:
                    em.op("vector", (lambda e, hp=hp, Lk=Lk: e.tensor_scalar(
                        out=avg[:, 0:Lk], in0=E_buf[hp][:, 0:Lk], scalar1=cvec[:],
                        scalar2=None, op0=ALU.mult)),
                        wait=[avg_free, ("V", V_c)], inc="V")
                else:
                    em.op("vector", (lambda e, hp=hp, Lk=Lk: e.scalar_tensor_tensor(
                        out=avg[:, 0:Lk], in0=E_buf[hp][:, 0:Lk], scalar=cvec[:],
                        in1=avg[:, 0:Lk], op0=ALU.mult, op1=ALU.add)),
                        wait=[("V", V_c)], inc="V")
                last_stt = em.now("V")
                E_free[hp] = ("V", last_stt)

            em.op("vector", (lambda e, ib=ib, Lk=Lk: e.scalar_tensor_tensor(
                out=l2_buf[ib][:, 0:Lk], in0=avg[:, 0:Lk], scalar=0.03125,
                in1=gum_b[ib][:, 0:Lk], op0=ALU.mult, op1=ALU.add)),
                wait=[l2_free[ib], (f"Lg{ib}", L_gum), ("V", last_stt)], inc="V")
            V_l2 = em.now("V")
            avg_free = ("V", V_l2)
            gum_free[ib] = ("V", V_l2)
            em.op("scalar", (lambda e, ib=ib, Lk=Lk: e.activation(
                out=w_buf[ib][:, 0:Lk], in_=l2_buf[ib][:, 0:Lk], func=AF.Exp,
                accum_out=Z2[:, ib:ib + 1])),
                wait=[("V", V_l2), w_free[ib]], inc="A")
            A_w = em.now("A")
            l2_free[ib] = ("A", A_w)
            em.op("vector", (lambda e, ib=ib: e.tensor_scalar(
                out=z01[:], in0=Z2[:, ib:ib + 1], scalar1=1e-37, scalar2=None,
                op0=ALU.max)),
                wait=[("A", A_w), ("V", last_comb), ("V", last_stt)], inc="V")
            em.op("vector", lambda e: e.reciprocal(out=rec2[:], in_=z01[:]),
                  wait=[("V", em.now("V"))], inc="V")
            V_rec2 = em.now("V")

            for j in range(i + 1):
                sl = j % 2
                em.op("tensor", (lambda e, ib=ib, j=j, sl=sl: e.matmul(
                    ptA[sl][:], w_buf[ib][:, j * 128:(j + 1) * 128], ident[:],
                    start=True, stop=True)),
                    wait=[("A", A_w), pt_free[sl]] + ([wT_free] if j == 0 else []),
                    inc="P")
                pv = em.now("P")
                em.op("vector", (lambda e, j=j, sl=sl: e.tensor_copy(
                    out=wT_sb[:, j, :], in_=ptA[sl][:])),
                    wait=[("P", pv)], inc="V")
                pt_free[sl] = ("V", em.now("V"))
            V_wT = em.now("V")
            w_free[ib] = ("P", em.now("P"))

            for j in range(i + 1):
                for n2 in range(2):
                    em.op("tensor", (lambda e, j=j, n2=n2, i=i: e.matmul(
                        sOut[:, n2 * 512:(n2 + 1) * 512],
                        wT_sb[:, j, :],
                        xn_a[:, j, n2 * 512:(n2 + 1) * 512],
                        start=(j == 0), stop=(j == i))),
                        wait=([("V", V_wT), sOut_slot_free[0], sOut_slot_free[1]]
                              if j == 0 and n2 == 0 else []),
                        inc="P" if (j == i and n2 == 1) else None)
            P_s = em.now("P")
            wT_free = ("P", P_s)

            em.op("vector", (lambda e, ib=ib: e.scalar_tensor_tensor(
                out=sg[:], in0=sOut[:], scalar=rec2[:], in1=gt_b[ib][:],
                op0=ALU.mult, op1=ALU.mult)),
                wait=[("P", P_s), (f"Lt{ib}", L_gt), ("V", V_rec2)], inc="V")
            if i == 0:
                em.op("vector", lambda e: e.memset(sg[0:1, :], 0.0),
                      wait=[("V", em.now("V"))], inc="V")
            em.op("vector", (lambda e, ib=ib: e.tensor_tensor(
                out=xres[ib][:], in0=xres[ib][:], in1=sg[:], op=ALU.add)),
                wait=[(f"Lr{ib}", L_xres), ("V", em.now("V"))], inc="V")
            V_comb = em.now("V")
            last_comb = V_comb
            sOut_slot_free[0] = ("V", V_comb)
            sOut_slot_free[1] = ("V", V_comb)
            gt_free[ib] = ("V", V_comb)
            em.op("gpsimd", (lambda e, b=b, i=i, ib=ib: e.dma_start(
                out=out_d[b, i * 128:(i + 1) * 128, :], in_=xres[ib][:])),
                wait=[("V", V_comb)], inc=f"So{ib}", inc_amt=16)
            xres_free[ib] = (f"So{ib}", em.now(f"So{ib}"))

        xn_a_lastuse = ("P", em.now("P"))

    em.op("gpsimd", None, wait=[("So0", em.now("So0")), ("So1", em.now("So1"))])

    sem_guards = []
    semh = {}
    for name in SEMS:
        g = nc.semaphore(f"sem_{name}")
        semh[name] = g.__enter__()
        sem_guards.append(g)
    blk = nc.Block()
    block = blk.__enter__()

    block.sync(lambda e: em.replay("sync", e, semh))
    block.gpsimd(lambda e: em.replay("gpsimd", e, semh))
    block.scalar(lambda e: em.replay("scalar", e, semh))
    block.vector(lambda e: em.replay("vector", e, semh))
    block.tensor(lambda e: em.replay("tensor", e, semh))

    blk.__exit__(None, None, None)
    for g in reversed(sem_guards):
        g.__exit__(None, None, None)
    for g in reversed(ps_guards):
        g.__exit__(None, None, None)
    return nc


_GUM = None
_NC = None


def _gumbel_rows(flat_idx):
    lo = flat_idx.astype(np.uint32)
    k0 = np.uint32(0)
    k1 = np.uint32(42)
    ks = (k0, k1, np.uint32(k0 ^ k1 ^ np.uint32(0x1BD11BDA)))
    rot0 = (13, 15, 26, 6)
    rot1 = (17, 29, 16, 24)
    with np.errstate(over="ignore"):
        x0 = np.full_like(lo, ks[0])
        x1 = lo + ks[1]

        def rounds(x0, x1, rots):
            for r in rots:
                x0 = x0 + x1
                x1 = (x1 << np.uint32(r)) | (x1 >> np.uint32(32 - r))
                x1 = x1 ^ x0
            return x0, x1

        x0, x1 = rounds(x0, x1, rot0); x0 = x0 + ks[1]; x1 = x1 + ks[2] + np.uint32(1)
        x0, x1 = rounds(x0, x1, rot1); x0 = x0 + ks[2]; x1 = x1 + ks[0] + np.uint32(2)
        x0, x1 = rounds(x0, x1, rot0); x0 = x0 + ks[0]; x1 = x1 + ks[1] + np.uint32(3)
        x0, x1 = rounds(x0, x1, rot1); x0 = x0 + ks[1]; x1 = x1 + ks[2] + np.uint32(4)
        x0, x1 = rounds(x0, x1, rot0); x0 = x0 + ks[2]; x1 = x1 + ks[0] + np.uint32(5)
        bits = x0 ^ x1
    f = (bits >> np.uint32(9)) | np.uint32(0x3F800000)
    fl = f.view(np.float32) - np.float32(1.0)
    minval = np.float32(1e-6)
    maxval = np.float32(1.0 - 1e-6)
    u = np.maximum(minval, fl * (maxval - minval) + minval)
    return -np.log(-np.log(u))


def _gumbel():
    global _GUM
    if _GUM is None:
        g = np.empty((32, T, T), np.float32)
        base = np.arange(T * T, dtype=np.uint64)
        for b in range(32):
            g[b] = (np.float32(0.5)
                    * _gumbel_rows(base + np.uint64(b) * np.uint64(T * T))
                    ).reshape(T, T)
        idx = np.arange(T)
        maskrow = idx[None, :] >= idx[:, None]
        g[:, maskrow] = NEG
        _GUM = g
    return _GUM


def kernel(L0_attn_hidden_states, L0_attn_in_proj_weight, L0_attn_in_proj_bias,
           L0_ffn_w_gate, L0_ffn_w_up, L0_ffn_w_down):
    global _NC
    x = np.ascontiguousarray(np.asarray(L0_attn_hidden_states, dtype=np.float32))
    wqkT = np.ascontiguousarray(np.asarray(L0_attn_in_proj_weight,
                                           dtype=np.float32)[:2 * E].T)
    biasqk = np.ascontiguousarray(np.asarray(L0_attn_in_proj_bias,
                                             dtype=np.float32)[:2 * E])
    wgT = np.ascontiguousarray(np.asarray(L0_ffn_w_gate, dtype=np.int32).T)
    wuT = np.ascontiguousarray(np.asarray(L0_ffn_w_up, dtype=np.int32).T)
    wdT = np.ascontiguousarray(np.asarray(L0_ffn_w_down, dtype=np.int32).T)
    gum = _gumbel()
    ident = np.eye(128, dtype=ml_dtypes.bfloat16)
    r = np.arange(128)
    maskneg = np.where(r[None, :] >= r[:, None], np.float32(NEG),
                       np.float32(0.0)).astype(ml_dtypes.bfloat16)
    ones1 = np.ones((1, 128), np.float32)
    if _NC is None:
        _NC = build_program()
    in_maps = []
    for c in range(NCORES):
        in_maps.append({
            "x": x[c * BPC:(c + 1) * BPC],
            "gum": gum[c * BPC:(c + 1) * BPC],
            "wqkT": wqkT, "biasqk": biasqk,
            "wgT": wgT, "wuT": wuT, "wdT": wdT,
            "ident": ident, "maskneg": maskneg, "ones1": ones1,
        })
    res = run_bass_kernel_spmd(_NC, in_maps, core_ids=list(range(NCORES)))
    return np.concatenate([r_["out"] for r_ in res.results], axis=0)


# version 03
# speedup vs baseline: 1.0734x; 1.0734x over previous
import numpy as np
import ml_dtypes
import concourse.bass as bass
import concourse.mybir as mybir
from concourse.bass_utils import run_bass_kernel_spmd

F32 = mybir.dt.float32
BF16 = mybir.dt.bfloat16
I32 = mybir.dt.int32
AF = mybir.ActivationFunctionType
ALU = mybir.AluOpType
AX = mybir.AxisListType

NCORES = 8
BPC = 4
T = 2048
E = 1024
HID = 3072
NHEAD = 16
D = 64
NEG = -1.0e9
NTILE = T // 128
FTILE = 256
NFT = (BPC * T) // FTILE

ENGS = ("sync", "gpsimd", "scalar", "vector", "tensor")


class Em:

    def __init__(self):
        self.ops = {e: [] for e in ENGS}
        self.cnt = {}

    def sem(self, *names):
        for n in names:
            self.cnt[n] = 0

    def op(self, eng, fn, wait=(), inc=None, inc_amt=1):
        w = tuple((s, v) for (s, v) in wait if v > 0)
        if inc is not None:
            self.cnt[inc] += inc_amt
        self.ops[eng].append((w, fn, inc, inc_amt))
        return self.cnt[inc] if inc is not None else None

    def now(self, sem):
        return self.cnt[sem]

    def replay(self, eng_name, engobj, semh):
        last = {}
        for (waits, fn, inc, amt) in self.ops[eng_name]:
            for (s, v) in waits:
                if last.get(s, 0) < v:
                    engobj.wait_ge(semh[s], v)
                    last[s] = v
            inst = fn(engobj) if fn is not None else None
            if inc is not None:
                assert inst is not None
                inst.then_inc(semh[inc], amt)


def build_program():
    nc = bass.Bass("TRN2", target_bir_lowering=False)

    x_d = nc.dram_tensor("x", [BPC, T, E], F32, kind="ExternalInput").ap()
    gum_d = nc.dram_tensor("gum", [BPC, T, T], F32, kind="ExternalInput").ap()
    wqkT_d = nc.dram_tensor("wqkT", [E, 2 * E], F32, kind="ExternalInput").ap()
    bias_d = nc.dram_tensor("biasqk", [2 * E], F32, kind="ExternalInput").ap()
    wgT_d = nc.dram_tensor("wgT", [E, HID], I32, kind="ExternalInput").ap()
    wuT_d = nc.dram_tensor("wuT", [E, HID], I32, kind="ExternalInput").ap()
    wdT_d = nc.dram_tensor("wdT", [HID, E], I32, kind="ExternalInput").ap()
    ident_d = nc.dram_tensor("ident", [128, 128], BF16, kind="ExternalInput").ap()
    mask_d = nc.dram_tensor("maskneg", [128, 128], BF16, kind="ExternalInput").ap()
    ones_d = nc.dram_tensor("ones1", [1, 128], F32, kind="ExternalInput").ap()
    out_d = nc.dram_tensor("out", [BPC, T, E], F32, kind="ExternalOutput").ap()
    xn_scr = nc.dram_tensor("xn_scr", [BPC * T, E], BF16, kind="Internal").ap()
    g_scr = nc.dram_tensor("g_scr", [BPC * T, E], BF16, kind="Internal").ap()
    wqk_scr = nc.dram_tensor("wqk_scr", [E, 2 * E], BF16, kind="Internal").ap()

    ARENA_BYTES = 209088
    arena_h = nc.alloc_sbuf_tensor("arena", [128, ARENA_BYTES], mybir.dt.uint8)
    ARBASE = nc.lookup_mls(arena_h).memorylocations[0].addr
    alloc = {}

    def sb(name, shape, dtype, off):
        h = nc.alloc_sbuf_tensor_at(name, shape, dtype, offset=ARBASE + off)
        alloc[name] = h
        return h.ap()

    GLOB0 = 0
    ident = sb("identsb", [128, 128], BF16, GLOB0 + 0)
    maskneg = sb("masksb", [128, 128], BF16, GLOB0 + 256)
    ones1 = sb("onessb", [1, 128], F32, GLOB0 + 512)
    biasS = sb("biasS", [128, 16], F32, GLOB0 + 1024)
    biasRaw = sb("biasRaw", [128, 16], F32, GLOB0 + 1088)
    bias1 = sb("bias1", [128, 1], F32, GLOB0 + 1152)
    smallA = sb("smallA", [128, 48], F32, GLOB0 + 1184)
    sc1 = sb("sc1", [1, 16], F32, GLOB0 + 1376)
    AR = 2048

    wg_bf = sb("wg_bf", [128, 8, HID], BF16, AR + 0)
    wu_bf = sb("wu_bf", [128, 8, HID], BF16, AR + 49152)
    wd_bf = sb("wd_bf", [128, 24, E], BF16, AR + 98304)
    x_f = [sb(f"x_f{j}", [128, 2, E], F32, AR + 147456 + j * 8192) for j in range(2)]
    xn_f = [sb(f"xn_f{j}", [128, 2, E], BF16, AR + 163840 + j * 4096) for j in range(2)]
    xnT_f = sb("xnT_f", [128, 8, FTILE], BF16, AR + 172032)
    a_strip = sb("a_strip", [128, FTILE], BF16, AR + 176128)
    u_strip = sb("u_strip", [128, FTILE], BF16, AR + 176640)
    h_bf = sb("h_bf", [128, 24, FTILE], BF16, AR + 177152)
    g_f = [sb(f"g_f{j}", [128, 2, E], BF16, AR + 189440 + j * 4096) for j in range(2)]
    junk = sb("junk", [128, E], BF16, AR + 197632)
    stg = [sb(f"stg{j}", [128, 1024], I32, AR + 147456 + j * 4096) for j in range(3)]
    wstgF = [sb(f"wstgF{j}", [128, 1024], F32, AR + 159744 + j * 8192)
             for j in range(2)]
    wqk_stg = [sb(f"wqkstg{j}", [128, 1024], BF16, AR + 163840 + j * 2048)
               for j in range(2)]

    wqk_bf = sb("wqk_bf", [128, 8, 2 * E], BF16, AR + 0)
    QT = sb("QT", [128, 8, T], BF16, AR + 32768)
    KT = sb("KT", [128, 8, T], BF16, AR + 65536)
    xn_a = sb("xn_a", [128, NTILE, E], BF16, AR + 98304)
    xnT_tb = sb("xnT_tb", [128, 8, 512], BF16, AR + 131072)
    avg = sb("avg", [128, T], F32, AR + 139264)
    E_buf = [sb(f"E_buf{j}", [128, T], BF16, AR + 147456 + j * 4096) for j in range(2)]
    l2_buf = [sb(f"l2_buf{j}", [128, T], BF16, AR + 155648 + j * 4096) for j in range(2)]
    w_buf = [sb(f"w_buf{j}", [128, T], BF16, AR + 163840 + j * 4096) for j in range(2)]
    wT_sb = sb("wT_sb", [128, NTILE, 128], BF16, AR + 172032)
    gum_b = [sb(f"gum_b{j}", [128, T], F32, AR + 176128 + j * 8192) for j in range(2)]
    xres = [sb(f"xres{j}", [128, E], F32, AR + 192512 + j * 4096) for j in range(2)]
    gt_b = [sb(f"gt_b{j}", [128, E], BF16, AR + 200704 + j * 2048) for j in range(2)]
    sg = sb("sg", [128, E], BF16, AR + 204800)
    Zp = sb("Zp", [128, 4], F32, AR + 206848)
    cvec = sb("cvec", [128, 1], F32, AR + 206880)
    z01 = sb("z01", [128, 1], F32, AR + 206912)
    Z2 = sb("Z2", [128, 2], F32, AR + 206944)
    rec2 = sb("rec2", [128, 1], F32, AR + 206976)

    ps_guards = []

    def ps(name, shape):
        g = nc.psum_tensor(name, shape, F32)
        h = g.__enter__()
        ps_guards.append(g)
        return h.ap()

    big0 = ps("big0", [128, 1024])
    big1 = ps("big1", [128, 1024])
    big2 = ps("big2", [128, 1024])
    small0 = ps("small0", [128, 512])
    small1 = ps("small1", [128, 512])
    psA = small0[:, 0:FTILE]
    psB = small1[:, 0:FTILE]
    psF = [big0, big1]
    ptF = [big2[:, 0:128], big2[:, 512:640]]
    region = [big0, big1]
    sOut = big2
    ptA = [small0[:, 0:128], small1[:, 0:128]]

    em = Em()
    SEMS = ("Lc", "Lx0", "Lx1", "Ls0", "Ls1", "Ls2", "Swq", "Lg0", "Lg1",
            "Lr0", "Lr1", "Lt0", "Lt1", "Lw0", "Lw1", "Lb",
            "Lq0", "Lq1", "Lq2", "Lq3", "Lx0a", "Lx1a", "Lx2a", "Lx3a",
            "Sxn0", "Sxn1", "Sg0", "Sg1", "So0", "So1", "P", "A", "V", "Pl")
    em.sem(*SEMS)

    em.op("sync", lambda e: e.dma_start(out=ident[:], in_=ident_d[:]), inc="Lc", inc_amt=16)
    em.op("sync", lambda e: e.dma_start(out=maskneg[:], in_=mask_d[:]), inc="Lc", inc_amt=16)
    em.op("sync", lambda e: e.dma_start(out=ones1[:], in_=ones_d[:]), inc="Lc", inc_amt=16)
    L_consts = em.now("Lc")

    QN = ("sync", "scalar", "gpsimd")
    cast_jobs = []
    for c in range(8):
        for h3 in range(3):
            cast_jobs.append((wgT_d[c * 128:(c + 1) * 128, h3 * 1024:(h3 + 1) * 1024],
                              wg_bf[:, c, h3 * 1024:(h3 + 1) * 1024]))
    for c in range(8):
        for h3 in range(3):
            cast_jobs.append((wuT_d[c * 128:(c + 1) * 128, h3 * 1024:(h3 + 1) * 1024],
                              wu_bf[:, c, h3 * 1024:(h3 + 1) * 1024]))
    for c in range(24):
        cast_jobs.append((wdT_d[c * 128:(c + 1) * 128, :], wd_bf[:, c, :]))
    stg_free = [("V", 0), ("V", 0), ("V", 0)]
    for j, (src, dst) in enumerate(cast_jobs):
        sl = j % 3
        em.op(QN[sl], (lambda e, s=src, t=stg[sl][:]: e.dma_start(out=t, in_=s)),
              wait=[stg_free[sl]], inc=f"Ls{sl}", inc_amt=16)
        lv = em.now(f"Ls{sl}")
        em.op("vector", (lambda e, s=stg[sl][:], t=dst: e.tensor_copy(out=t, in_=s)),
              wait=[(f"Ls{sl}", lv)], inc="V")
        stg_free[sl] = ("V", em.now("V"))
    wstg_free = [("V", 0), ("V", 0)]
    wqs_free = [("Swq", 0), ("Swq", 0)]
    for j in range(16):
        c, h = j // 2, j % 2
        sl = j % 2
        qn = QN[sl]
        em.op(qn, (lambda e, c=c, h=h, sl=sl: e.dma_start(
            out=wstgF[sl][:],
            in_=wqkT_d[c * 128:(c + 1) * 128, h * 1024:(h + 1) * 1024])),
            wait=[wstg_free[sl]], inc=f"Lw{sl}", inc_amt=16)
        lv = em.now(f"Lw{sl}")
        em.op("vector", (lambda e, sl=sl: e.tensor_copy(
            out=wqk_stg[sl][:], in_=wstgF[sl][:])),
            wait=[(f"Lw{sl}", lv), wqs_free[sl]], inc="V")
        wstg_free[sl] = ("V", em.now("V"))
        em.op(qn, (lambda e, c=c, h=h, sl=sl: e.dma_start(
            out=wqk_scr[c * 128:(c + 1) * 128, h * 1024:(h + 1) * 1024],
            in_=wqk_stg[sl][:])),
            wait=[("V", em.now("V"))], inc="Swq", inc_amt=16)
        wqs_free[sl] = ("Swq", em.now("Swq"))
    V_wcast = em.now("V")
    S_wq = em.now("Swq")

    xf_free = [("V", 0), ("V", 0)]
    gf_free = [("Sg0", 0), ("Sg1", 0)]
    xn_free = [("Sxn0", 0), ("Sxn1", 0)]
    psF_free = ("A", 0)
    psAB_free = [("A", 0), ("A", 0)]
    h_free = ("P", 0)
    au_free = ("V", 0)
    ptF_free = [("V", 0), ("V", 0)]
    NSUB = FTILE // 128

    for ft in range(NFT):
        b = ft // (T // FTILE)
        t0 = (ft % (T // FTILE)) * FTILE
        xb = ft % 2
        w_extra = [("V", V_wcast), ("Swq", S_wq)] if ft < 2 else []
        em.op("sync", (lambda e, b=b, t0=t0, xb=xb:
                       e.dma_start(out=x_f[xb][:],
                                   in_=x_d[b, t0:t0 + FTILE, :].rearrange("(s p) e -> p s e", p=128))),
              wait=[xf_free[xb]] + w_extra, inc=f"Lx{xb}", inc_amt=16)
        L_x = em.now(f"Lx{xb}")

        def vop(fn, wait=()):
            nonlocal vprev
            em.op("vector", fn, wait=list(wait) + [("V", vprev)], inc="V")
            vprev = em.now("V")
        vprev = em.now("V")
        wv = [(f"Lx{xb}", L_x)] + ([("V", V_wcast)] if ft < 2 else []) + [xn_free[xb]]
        m_v = smallA[:, 0:2]
        q2_v = smallA[:, 2:4]
        var_v = smallA[:, 4:6]
        r_v = smallA[:, 6:8]
        tmp_v = smallA[:, 8:10]
        tmp2_v = smallA[:, 10:12]
        for s in range(NSUB):
            vop((lambda e, s=s, xb=xb: e.reduce_sum(
                out=m_v[:, s:s + 1], in_=x_f[xb][:, s, :], axis=AX.X)), wait=wv)
            wv = []
            vop((lambda e, s=s, xb=xb: e.scalar_tensor_tensor(
                out=junk[:], in0=x_f[xb][:, s, :], scalar=1.0, in1=x_f[xb][:, s, :],
                op0=ALU.mult, op1=ALU.mult, accum_out=q2_v[:, s:s + 1])))
        vop(lambda e: e.tensor_scalar(
            out=m_v[:], in0=m_v[:], scalar1=1.0 / E, scalar2=None, op0=ALU.mult))
        vop(lambda e: e.tensor_tensor(
            out=tmp_v[:], in0=m_v[:], in1=m_v[:], op=ALU.mult))
        vop(lambda e: e.tensor_scalar(
            out=var_v[:], in0=q2_v[:], scalar1=1.0 / E, scalar2=None, op0=ALU.mult))
        vop(lambda e: e.tensor_tensor(
            out=var_v[:], in0=var_v[:], in1=tmp_v[:], op=ALU.subtract))
        vop(lambda e: e.tensor_scalar(
            out=var_v[:], in0=var_v[:], scalar1=1e-5, scalar2=None, op0=ALU.add))
        vop(lambda e: e.memset(r_v[:], 1.0))
        for _ in range(5):
            vop(lambda e: e.tensor_tensor(
                out=tmp_v[:], in0=var_v[:], in1=r_v[:], op=ALU.mult))
            vop(lambda e: e.tensor_tensor(
                out=tmp2_v[:], in0=tmp_v[:], in1=r_v[:], op=ALU.mult))
            vop(lambda e: e.tensor_scalar(
                out=tmp2_v[:], in0=tmp2_v[:], scalar1=-0.5, scalar2=1.5,
                op0=ALU.mult, op1=ALU.add))
            vop(lambda e: e.tensor_tensor(
                out=r_v[:], in0=r_v[:], in1=tmp2_v[:], op=ALU.mult))
        for s in range(NSUB):
            vop((lambda e, s=s, xb=xb: e.tensor_scalar(
                out=xn_f[xb][:, s, :], in0=x_f[xb][:, s, :],
                scalar1=m_v[:, s:s + 1], scalar2=r_v[:, s:s + 1],
                op0=ALU.subtract, op1=ALU.mult)))
        V_xn = em.now("V")
        xf_free[xb] = ("V", V_xn)

        em.op("gpsimd", (lambda e, b=b, t0=t0, xb=xb: e.dma_start(
            out=xn_scr[b * T + t0:b * T + t0 + FTILE, :].rearrange("(s p) e -> p s e", p=128),
            in_=xn_f[xb][:])), wait=[("V", V_xn)], inc=f"Sxn{xb}", inc_amt=16)
        xn_free[xb] = (f"Sxn{xb}", em.now(f"Sxn{xb}"))

        for s in range(NSUB):
            for c in range(8):
                k = s * 8 + c
                sl = k % 2
                em.op("tensor", (lambda e, s=s, c=c, sl=sl, xb=xb: e.matmul(
                    ptF[sl][:], xn_f[xb][:, s, c * 128:(c + 1) * 128], ident[:],
                    start=True, stop=True)),
                    wait=[("V", V_xn), ptF_free[sl], ("Lc", 48)], inc="P")
                pv = em.now("P")
                em.op("vector", (lambda e, s=s, c=c, sl=sl: e.tensor_copy(
                    out=xnT_f[:, c, s * 128:(s + 1) * 128], in_=ptF[sl][:])),
                    wait=[("P", pv)], inc="V")
                ptF_free[sl] = ("V", em.now("V"))
        V_xnT = em.now("V")

        for m in range(24):
            for ec in range(8):
                em.op("tensor", (lambda e, m=m, ec=ec: e.matmul(
                    psA[:], wg_bf[:, ec, m * 128:(m + 1) * 128], xnT_f[:, ec, :],
                    start=(ec == 0), stop=(ec == 7))),
                    wait=([("V", V_xnT), psAB_free[0]] if ec == 0 else []),
                    inc="P" if ec == 7 else None)
            P_gate = em.now("P")
            for ec in range(8):
                em.op("tensor", (lambda e, m=m, ec=ec: e.matmul(
                    psB[:], wu_bf[:, ec, m * 128:(m + 1) * 128], xnT_f[:, ec, :],
                    start=(ec == 0), stop=(ec == 7))),
                    wait=([psAB_free[1]] if ec == 0 else []),
                    inc="P" if ec == 7 else None)
            P_up = em.now("P")
            em.op("scalar", (lambda e: e.activation(
                out=a_strip[:], in_=psA[:], func=AF.Silu, scale=1.0 / 127.0)),
                wait=[("P", P_gate), au_free], inc="A")
            psAB_free[0] = ("A", em.now("A"))
            em.op("scalar", (lambda e: e.activation(
                out=u_strip[:], in_=psB[:], func=AF.Copy)),
                wait=[("P", P_up)], inc="A")
            psAB_free[1] = ("A", em.now("A"))
            A_u = em.now("A")
            em.op("vector", (lambda e, m=m: e.tensor_tensor(
                out=h_bf[:, m, :], in0=a_strip[:], in1=u_strip[:], op=ALU.mult)),
                wait=[("A", A_u), (h_free if m == 0 else ("P", 0))], inc="V")
            au_free = ("V", em.now("V"))
        V_h = em.now("V")

        for m in range(24):
            vm = V_h - 23 + m
            for s2 in range(NSUB):
                for n2 in range(2):
                    em.op("tensor", (lambda e, m=m, s2=s2, n2=n2: e.matmul(
                        psF[s2][:, n2 * 512:(n2 + 1) * 512],
                        h_bf[:, m, s2 * 128:(s2 + 1) * 128],
                        wd_bf[:, m, n2 * 512:(n2 + 1) * 512],
                        start=(m == 0), stop=(m == 23))),
                        wait=([("V", vm)] + ([psF_free] if m == 0 and s2 == 0 and n2 == 0 else [])),
                        inc="P" if (m == 23 and s2 == NSUB - 1 and n2 == 1) else None)
        P_down = em.now("P")
        h_free = ("P", P_down)

        gb = ft % 2
        for s2 in range(NSUB):
            em.op("scalar", (lambda e, s2=s2, gb=gb: e.activation(
                out=g_f[gb][:, s2, :], in_=psF[s2][:], func=AF.Silu,
                scale=1.0 / (127.0 * 127.0))),
                wait=[("P", P_down)] + ([gf_free[gb]] if s2 == 0 else []),
                inc="A" if s2 == NSUB - 1 else None)
        A_g = em.now("A")
        psF_free = ("A", A_g)
        em.op("gpsimd", (lambda e, b=b, t0=t0, gb=gb: e.dma_start(
            out=g_scr[b * T + t0:b * T + t0 + FTILE, :].rearrange("(s p) e -> p s e", p=128),
            in_=g_f[gb][:])), wait=[("A", A_g)], inc=f"Sg{gb}", inc_amt=16)
        gf_free[gb] = (f"Sg{gb}", em.now(f"Sg{gb}"))

    Fbar = [("P", em.now("P")), ("A", em.now("A")), ("V", em.now("V")),
            ("Sxn0", em.now("Sxn0")), ("Sxn1", em.now("Sxn1")),
            ("Sg0", em.now("Sg0")), ("Sg1", em.now("Sg1"))]

    AQ = ("sync", "scalar", "gpsimd")
    wqk_waits = []
    for c2 in range(4):
        qn = AQ[c2 % 3]
        em.op(qn, (lambda e, c2=c2: e.dma_start(
            out=wqk_bf[:, 2 * c2:2 * c2 + 2, :],
            in_=wqk_scr[c2 * 256:(c2 + 1) * 256, :].rearrange(
                "(c p) n -> p c n", p=128))),
            wait=Fbar + [("Swq", S_wq)], inc=f"Lq{c2}", inc_amt=16)
        wqk_waits.append((f"Lq{c2}", em.now(f"Lq{c2}")))
    for c in range(16):
        em.op("sync", (lambda e, c=c: e.dma_start(
            out=biasRaw[:, c:c + 1],
            in_=bias_d[c * 128:(c + 1) * 128].rearrange("(p o) -> p o", o=1))),
            inc="Lb", inc_amt=16)
    L_bias = em.now("Lb")
    em.op("vector", lambda e: e.tensor_scalar(
        out=biasS[:, 0:8], in0=biasRaw[:, 0:8], scalar1=0.125, scalar2=None, op0=ALU.mult),
        wait=[("Lb", L_bias)] + Fbar)
    em.op("vector", lambda e: e.tensor_copy(out=biasS[:, 8:16], in_=biasRaw[:, 8:16]),
          inc="V")
    V_bias = em.now("V")

    xn_a_lastuse = ("P", 0)
    sOut_slot_free = [("V", 0), ("V", 0)]
    pt_free = [("V", 0), ("V", 0)]
    regions_free = [("A", 0), ("A", 0)]
    E_free = [("V", 0), ("V", 0)]
    l2_free = [("A", 0), ("A", 0)]
    w_free = [("P", 0), ("P", 0)]
    wT_free = ("P", 0)
    gum_free = [("V", 0), ("V", 0)]
    xres_free = [("S", 0), ("S", 0)]
    gt_free = [("V", 0), ("V", 0)]
    avg_free = ("V", 0)
    last_stt = 0
    last_comb = 0

    for b in range(BPC):
        xn_tile_w = []
        for tt in range(NTILE):
            qn = AQ[tt % 3]
            em.op(qn, (lambda e, b=b, tt=tt: e.dma_start(
                out=xn_a[:, tt, :],
                in_=xn_scr[b * T + tt * 128:b * T + (tt + 1) * 128, :])),
                wait=[xn_a_lastuse] + (Fbar if b == 0 else []),
                inc=f"Lx{tt % 3}a", inc_amt=16)
            xn_tile_w.append((f"Lx{tt % 3}a", em.now(f"Lx{tt % 3}a")))

        for tb in range(4):
            for k in range(32):
                tt = tb * 4 + k // 8
                ec = k % 8
                sl = k % 2
                em.op("tensor", (lambda e, tt=tt, ec=ec, sl=sl: e.matmul(
                    ptA[sl][:], xn_a[:, tt, ec * 128:(ec + 1) * 128], ident[:],
                    start=True, stop=True)),
                    wait=[xn_tile_w[tt], pt_free[sl]],
                    inc="P")
                pv = em.now("P")
                em.op("vector", (lambda e, tt=tt, ec=ec, sl=sl: e.tensor_copy(
                    out=xnT_tb[:, ec, (tt % 4) * 128:(tt % 4 + 1) * 128], in_=ptA[sl][:])),
                    wait=[("P", pv)], inc="V")
                pt_free[sl] = ("V", em.now("V"))
            V_tb = em.now("V")
            for co in range(16):
                sl = co % 2
                for ec in range(8):
                    em.op("tensor", (lambda e, co=co, ec=ec, sl=sl: e.matmul(
                        sOut[:, sl * 512:(sl + 1) * 512],
                        wqk_bf[:, ec, co * 128:(co + 1) * 128],
                        xnT_tb[:, ec, :],
                        start=(ec == 0), stop=(ec == 7))),
                        wait=([("V", V_tb), sOut_slot_free[sl]] + wqk_waits
                              if ec == 0 else []),
                        inc="P" if ec == 7 else None)
                P_co = em.now("P")
                dstQK = QT if co < 8 else KT
                cc = co % 8
                scale = 0.125 if co < 8 else 1.0
                em.op("scalar", (lambda e, co=co, sl=sl, dstQK=dstQK, cc=cc, tb=tb, scale=scale:
                                 e.activation(
                    out=dstQK[:, cc, tb * 512:(tb + 1) * 512],
                    in_=sOut[:, sl * 512:(sl + 1) * 512], func=AF.Identity,
                    bias=biasS[:, co:co + 1], scale=scale)),
                    wait=[("P", P_co), ("V", V_bias)], inc="A")
                sOut_slot_free[sl] = ("A", em.now("A"))
        A_proj = em.now("A")

        qm = smallA[:, 16:17]
        km = smallA[:, 17:18]
        qm_bf = smallA[:, 18:19].bitcast(BF16)[:, 0:1]
        km_bf = smallA[:, 20:21].bitcast(BF16)[:, 0:1]
        def nvop(fn, wait=()):
            nonlocal nbprev
            em.op("vector", fn, wait=list(wait) + [("V", nbprev)], inc="V")
            nbprev = em.now("V")
        nbprev = em.now("V")
        nvop(lambda e: e.reduce_max(
            out=qm[:], in_=QT.rearrange("p c t -> p (c t)"), axis=AX.X,
            apply_absolute_value=True), wait=[("A", A_proj)])
        nvop(lambda e: e.reduce_max(
            out=km[:], in_=KT.rearrange("p c t -> p (c t)"), axis=AX.X,
            apply_absolute_value=True))
        nvop(lambda e: e.tensor_copy(out=qm_bf[:], in_=qm[:]))
        nvop(lambda e: e.tensor_copy(out=km_bf[:], in_=km[:]))
        V_qk = em.now("V")
        em.op("tensor", lambda e: e.matmul(ptA[0][0:1, :], qm_bf[:], ident[:],
                                           start=True, stop=True),
              wait=[("V", V_qk), pt_free[0]], inc="P")
        P_q = em.now("P")
        em.op("tensor", lambda e: e.matmul(ptA[1][0:1, :], km_bf[:], ident[:],
                                           start=True, stop=True),
              wait=[pt_free[1]], inc="P")
        P_k = em.now("P")
        nvop(lambda e: e.reduce_max(out=sc1[:, 0:1], in_=ptA[0][0:1, :], axis=AX.X),
             wait=[("P", P_q)])
        nvop(lambda e: e.reduce_max(out=sc1[:, 1:2], in_=ptA[1][0:1, :], axis=AX.X),
             wait=[("P", P_k)])
        nvop(lambda e: e.tensor_tensor(
            out=sc1[:, 2:3], in0=sc1[:, 0:1], in1=sc1[:, 1:2], op=ALU.mult))
        nvop(lambda e: e.tensor_scalar(
            out=sc1[:, 2:3], in0=sc1[:, 2:3], scalar1=64.0, scalar2=80.0,
            op0=ALU.mult, op1=ALU.min))
        nvop(lambda e: e.tensor_scalar(
            out=sc1[:, 2:3], in0=sc1[:, 2:3], scalar1=-1.0, scalar2=None, op0=ALU.mult))
        V_nb = em.now("V")
        pt_free[0] = ("V", V_nb)
        pt_free[1] = ("V", V_nb)
        em.op("tensor", lambda e: e.matmul(ptA[0][:, 0:1], ones1[:], sc1[:, 2:3],
                                           start=True, stop=True),
              wait=[("V", V_nb)], inc="P")
        P_b = em.now("P")
        em.op("vector", lambda e: e.tensor_copy(out=bias1[:], in_=ptA[0][:, 0:1]),
              wait=[("P", P_b)], inc="V")
        pt_free[0] = ("V", em.now("V"))
        V_bias1 = em.now("V")

        for i in range(NTILE):
            Lk = 128 * (i + 1)
            ib = i % 2
            Lh = Lk // 2
            em.op("sync", (lambda e, b=b, i=i, Lh=Lh, ib=ib: e.dma_start(
                out=gum_b[ib][:, 0:Lh], in_=gum_d[b, i * 128:(i + 1) * 128, 0:Lh])),
                wait=[gum_free[ib]], inc=f"Lg{ib}", inc_amt=16)
            em.op("scalar", (lambda e, b=b, i=i, Lh=Lh, Lk=Lk, ib=ib: e.dma_start(
                out=gum_b[ib][:, Lh:Lk], in_=gum_d[b, i * 128:(i + 1) * 128, Lh:Lk])),
                wait=[gum_free[ib]], inc=f"Lg{ib}", inc_amt=16)
            L_gum = em.now(f"Lg{ib}")
            em.op("sync", (lambda e, b=b, i=i, ib=ib: e.dma_start(
                out=xres[ib][:], in_=x_d[b, i * 128:(i + 1) * 128, :])),
                wait=[xres_free[ib]], inc=f"Lr{ib}", inc_amt=16)
            L_xres = em.now(f"Lr{ib}")
            em.op("sync", (lambda e, b=b, i=i, ib=ib: e.dma_start(
                out=gt_b[ib][:], in_=g_scr[b * T + i * 128:b * T + (i + 1) * 128, :])),
                wait=[gt_free[ib]], inc=f"Lt{ib}", inc_amt=16)
            L_gt = em.now(f"Lt{ib}")

            n_d = (Lk - 128) // 512
            d_off = (Lk - 128) - 512 * n_d
            nchunks = (Lk + 511) // 512
            for h in range(NHEAD):
                hp = h % 2
                R = region[hp]
                qb = 64 * (h % 2)
                qc = h // 2
                parts = [(0, min(Lk, 1024))] + ([(1024, Lk)] if Lk > 1024 else [])
                for (pi, (c0, c1)) in enumerate(parts):
                    nlo = c0 // 512
                    nhi = (c1 + 511) // 512
                    for n in range(nlo, nhi):
                        k0 = 512 * n
                        w_n = min(512, Lk - k0)
                        rs = k0 - c0
                        is_diag = (n == n_d)
                        last = (n == nhi - 1)
                        wv = []
                        if n == nlo:
                            wv.append(regions_free[hp])
                            if h == 0 and pi == 0:
                                wv.append(("A", A_proj))
                        em.op("tensor", (lambda e, qb=qb, qc=qc, i=i, k0=k0, w_n=w_n, rs=rs,
                                         R=R, is_diag=is_diag: e.matmul(
                            R[:, rs:rs + w_n],
                            QT[qb:qb + 64, qc, i * 128:(i + 1) * 128],
                            KT[qb:qb + 64, qc, k0:k0 + w_n],
                            start=True, stop=not is_diag)),
                            wait=wv,
                            inc="P" if (last and not is_diag) else None)
                        if is_diag:
                            em.op("tensor", (lambda e, R=R, rs=rs, w_n=w_n: e.matmul(
                                R[:, rs + w_n - 128:rs + w_n],
                                ident[:], maskneg[:],
                                start=False, stop=True)),
                                inc="P" if last else None)
                    P_sc = em.now("P")
                    zc = hp * 2 + pi
                    em.op("scalar", (lambda e, R=R, c0=c0, c1=c1, hp=hp, zc=zc: e.activation(
                        out=E_buf[hp][:, c0:c1], in_=R[:, 0:c1 - c0], func=AF.Exp,
                        bias=bias1[:], scale=1.0,
                        accum_out=Zp[:, zc:zc + 1])),
                        wait=[("P", P_sc), ("V", V_bias1)] + ([E_free[hp]] if pi == 0 else []),
                        inc="A")
                    regions_free[hp] = ("A", em.now("A"))
                A_exp = em.now("A")
                if len(parts) == 2:
                    em.op("vector", (lambda e, hp=hp: e.tensor_tensor(
                        out=z01[:], in0=Zp[:, 2 * hp:2 * hp + 1],
                        in1=Zp[:, 2 * hp + 1:2 * hp + 2], op=ALU.add)),
                        wait=[("A", A_exp), ("V", last_stt)], inc="V")
                    em.op("vector", lambda e: e.reciprocal(out=cvec[:], in_=z01[:]),
                          wait=[("V", em.now("V"))], inc="V")
                elif i == 0:
                    em.op("vector", (lambda e, hp=hp: e.tensor_scalar(
                        out=z01[:], in0=Zp[:, 2 * hp:2 * hp + 1], scalar1=1e-37,
                        scalar2=None, op0=ALU.max)),
                        wait=[("A", A_exp), ("V", last_stt)], inc="V")
                    em.op("vector", lambda e: e.reciprocal(out=cvec[:], in_=z01[:]),
                          wait=[("V", em.now("V"))], inc="V")
                else:
                    em.op("vector", (lambda e, hp=hp: e.reciprocal(
                        out=cvec[:], in_=Zp[:, 2 * hp:2 * hp + 1])),
                        wait=[("A", A_exp), ("V", last_stt)], inc="V")
                V_c = em.now("V")
                if h == 0:
                    em.op("vector", (lambda e, hp=hp, Lk=Lk: e.tensor_scalar(
                        out=avg[:, 0:Lk], in0=E_buf[hp][:, 0:Lk], scalar1=cvec[:],
                        scalar2=None, op0=ALU.mult)),
                        wait=[avg_free, ("V", V_c)], inc="V")
                else:
                    em.op("vector", (lambda e, hp=hp, Lk=Lk: e.scalar_tensor_tensor(
                        out=avg[:, 0:Lk], in0=E_buf[hp][:, 0:Lk], scalar=cvec[:],
                        in1=avg[:, 0:Lk], op0=ALU.mult, op1=ALU.add)),
                        wait=[("V", V_c)], inc="V")
                last_stt = em.now("V")
                E_free[hp] = ("V", last_stt)

            em.op("vector", (lambda e, ib=ib, Lk=Lk: e.scalar_tensor_tensor(
                out=l2_buf[ib][:, 0:Lk], in0=avg[:, 0:Lk], scalar=0.03125,
                in1=gum_b[ib][:, 0:Lk], op0=ALU.mult, op1=ALU.add)),
                wait=[l2_free[ib], (f"Lg{ib}", L_gum), ("V", last_stt)], inc="V")
            V_l2 = em.now("V")
            avg_free = ("V", V_l2)
            gum_free[ib] = ("V", V_l2)
            em.op("scalar", (lambda e, ib=ib, Lk=Lk: e.activation(
                out=w_buf[ib][:, 0:Lk], in_=l2_buf[ib][:, 0:Lk], func=AF.Exp,
                accum_out=Z2[:, ib:ib + 1])),
                wait=[("V", V_l2), w_free[ib]], inc="A")
            A_w = em.now("A")
            l2_free[ib] = ("A", A_w)
            if i == 0:
                em.op("vector", (lambda e, ib=ib: e.tensor_scalar(
                    out=z01[:], in0=Z2[:, ib:ib + 1], scalar1=1e-37, scalar2=None,
                    op0=ALU.max)),
                    wait=[("A", A_w), ("V", last_comb), ("V", last_stt)], inc="V")
                em.op("vector", lambda e: e.reciprocal(out=rec2[:], in_=z01[:]),
                      wait=[("V", em.now("V"))], inc="V")
            else:
                em.op("vector", (lambda e, ib=ib: e.reciprocal(
                    out=rec2[:], in_=Z2[:, ib:ib + 1])),
                    wait=[("A", A_w), ("V", last_comb), ("V", last_stt)], inc="V")
            V_rec2 = em.now("V")

            for j in range(i + 1):
                sl = j % 2
                em.op("tensor", (lambda e, ib=ib, j=j, sl=sl: e.matmul(
                    ptA[sl][:], w_buf[ib][:, j * 128:(j + 1) * 128], ident[:],
                    start=True, stop=True)),
                    wait=[("A", A_w), pt_free[sl]] + ([wT_free] if j == 0 else []),
                    inc="P")
                pv = em.now("P")
                em.op("vector", (lambda e, j=j, sl=sl: e.tensor_copy(
                    out=wT_sb[:, j, :], in_=ptA[sl][:])),
                    wait=[("P", pv)], inc="V")
                pt_free[sl] = ("V", em.now("V"))
            V_wT = em.now("V")
            w_free[ib] = ("P", em.now("P"))

            for j in range(i + 1):
                for n2 in range(2):
                    em.op("tensor", (lambda e, j=j, n2=n2, i=i: e.matmul(
                        sOut[:, n2 * 512:(n2 + 1) * 512],
                        wT_sb[:, j, :],
                        xn_a[:, j, n2 * 512:(n2 + 1) * 512],
                        start=(j == 0), stop=(j == i))),
                        wait=([("V", V_wT), sOut_slot_free[0], sOut_slot_free[1]]
                              if j == 0 and n2 == 0 else []),
                        inc="P" if (j == i and n2 == 1) else None)
            P_s = em.now("P")
            wT_free = ("P", P_s)

            em.op("vector", (lambda e, ib=ib: e.scalar_tensor_tensor(
                out=sg[:], in0=sOut[:], scalar=rec2[:], in1=gt_b[ib][:],
                op0=ALU.mult, op1=ALU.mult)),
                wait=[("P", P_s), (f"Lt{ib}", L_gt), ("V", V_rec2)], inc="V")
            if i == 0:
                em.op("vector", lambda e: e.memset(sg[0:1, :], 0.0),
                      wait=[("V", em.now("V"))], inc="V")
            em.op("vector", (lambda e, ib=ib: e.tensor_tensor(
                out=xres[ib][:], in0=xres[ib][:], in1=sg[:], op=ALU.add)),
                wait=[(f"Lr{ib}", L_xres), ("V", em.now("V"))], inc="V")
            V_comb = em.now("V")
            last_comb = V_comb
            sOut_slot_free[0] = ("V", V_comb)
            sOut_slot_free[1] = ("V", V_comb)
            gt_free[ib] = ("V", V_comb)
            em.op("gpsimd", (lambda e, b=b, i=i, ib=ib: e.dma_start(
                out=out_d[b, i * 128:(i + 1) * 128, :], in_=xres[ib][:])),
                wait=[("V", V_comb)], inc=f"So{ib}", inc_amt=16)
            xres_free[ib] = (f"So{ib}", em.now(f"So{ib}"))

        xn_a_lastuse = ("P", em.now("P"))

    em.op("gpsimd", None, wait=[("So0", em.now("So0")), ("So1", em.now("So1"))])

    sem_guards = []
    semh = {}
    for name in SEMS:
        g = nc.semaphore(f"sem_{name}")
        semh[name] = g.__enter__()
        sem_guards.append(g)
    blk = nc.Block()
    block = blk.__enter__()

    block.sync(lambda e: em.replay("sync", e, semh))
    block.gpsimd(lambda e: em.replay("gpsimd", e, semh))
    block.scalar(lambda e: em.replay("scalar", e, semh))
    block.vector(lambda e: em.replay("vector", e, semh))
    block.tensor(lambda e: em.replay("tensor", e, semh))

    blk.__exit__(None, None, None)
    for g in reversed(sem_guards):
        g.__exit__(None, None, None)
    for g in reversed(ps_guards):
        g.__exit__(None, None, None)
    return nc


_GUM = None
_NC = None


def _gumbel_rows(flat_idx):
    lo = flat_idx.astype(np.uint32)
    k0 = np.uint32(0)
    k1 = np.uint32(42)
    ks = (k0, k1, np.uint32(k0 ^ k1 ^ np.uint32(0x1BD11BDA)))
    rot0 = (13, 15, 26, 6)
    rot1 = (17, 29, 16, 24)
    with np.errstate(over="ignore"):
        x0 = np.full_like(lo, ks[0])
        x1 = lo + ks[1]

        def rounds(x0, x1, rots):
            for r in rots:
                x0 = x0 + x1
                x1 = (x1 << np.uint32(r)) | (x1 >> np.uint32(32 - r))
                x1 = x1 ^ x0
            return x0, x1

        x0, x1 = rounds(x0, x1, rot0); x0 = x0 + ks[1]; x1 = x1 + ks[2] + np.uint32(1)
        x0, x1 = rounds(x0, x1, rot1); x0 = x0 + ks[2]; x1 = x1 + ks[0] + np.uint32(2)
        x0, x1 = rounds(x0, x1, rot0); x0 = x0 + ks[0]; x1 = x1 + ks[1] + np.uint32(3)
        x0, x1 = rounds(x0, x1, rot1); x0 = x0 + ks[1]; x1 = x1 + ks[2] + np.uint32(4)
        x0, x1 = rounds(x0, x1, rot0); x0 = x0 + ks[2]; x1 = x1 + ks[0] + np.uint32(5)
        bits = x0 ^ x1
    f = (bits >> np.uint32(9)) | np.uint32(0x3F800000)
    fl = f.view(np.float32) - np.float32(1.0)
    minval = np.float32(1e-6)
    maxval = np.float32(1.0 - 1e-6)
    u = np.maximum(minval, fl * (maxval - minval) + minval)
    return -np.log(-np.log(u))


def _gumbel():
    global _GUM
    if _GUM is None:
        g = np.empty((32, T, T), np.float32)
        base = np.arange(T * T, dtype=np.uint64)
        for b in range(32):
            g[b] = (np.float32(0.5)
                    * _gumbel_rows(base + np.uint64(b) * np.uint64(T * T))
                    ).reshape(T, T)
        idx = np.arange(T)
        maskrow = idx[None, :] >= idx[:, None]
        g[:, maskrow] = NEG
        _GUM = g
    return _GUM


def kernel(L0_attn_hidden_states, L0_attn_in_proj_weight, L0_attn_in_proj_bias,
           L0_ffn_w_gate, L0_ffn_w_up, L0_ffn_w_down):
    global _NC
    x = np.ascontiguousarray(np.asarray(L0_attn_hidden_states, dtype=np.float32))
    wqkT = np.ascontiguousarray(np.asarray(L0_attn_in_proj_weight,
                                           dtype=np.float32)[:2 * E].T)
    biasqk = np.ascontiguousarray(np.asarray(L0_attn_in_proj_bias,
                                             dtype=np.float32)[:2 * E])
    wgT = np.ascontiguousarray(np.asarray(L0_ffn_w_gate, dtype=np.int32).T)
    wuT = np.ascontiguousarray(np.asarray(L0_ffn_w_up, dtype=np.int32).T)
    wdT = np.ascontiguousarray(np.asarray(L0_ffn_w_down, dtype=np.int32).T)
    gum = _gumbel()
    ident = np.eye(128, dtype=ml_dtypes.bfloat16)
    r = np.arange(128)
    maskneg = np.where(r[None, :] >= r[:, None], np.float32(NEG),
                       np.float32(0.0)).astype(ml_dtypes.bfloat16)
    ones1 = np.ones((1, 128), np.float32)
    if _NC is None:
        _NC = build_program()
    in_maps = []
    for c in range(NCORES):
        in_maps.append({
            "x": x[c * BPC:(c + 1) * BPC],
            "gum": gum[c * BPC:(c + 1) * BPC],
            "wqkT": wqkT, "biasqk": biasqk,
            "wgT": wgT, "wuT": wuT, "wdT": wdT,
            "ident": ident, "maskneg": maskneg, "ones1": ones1,
        })
    res = run_bass_kernel_spmd(_NC, in_maps, core_ids=list(range(NCORES)))
    return np.concatenate([r_["out"] for r_ in res.results], axis=0)
